# version 02
# baseline (speedup 1.0000x reference)
import numpy as np
from contextlib import ExitStack

import concourse.bass as bass
import concourse.mybir as mybir
from concourse.bass_utils import run_bass_kernel_spmd
from concourse.alu_op_type import AluOpType as OP

F32 = mybir.dt.float32
BF16 = mybir.dt.bfloat16
I32 = mybir.dt.int32
AF = mybir.ActivationFunctionType
AX = mybir.AxisListType

TOK = 2048
HALO = 128
TT = TOK + HALO
PITCH = TT
NDC = 8
NMC = 32
LN_EPS = 1e-5
PI = float(np.pi)

SUBS_FULL = [(0, 512), (512, 512), (1024, 512), (1536, 512), (2048, 128)]
SUBS_OWN = [(0, 512), (512, 512), (1024, 512), (1536, 512)]

B_XN = 0
B_PH = 17408
B_PH2 = 21760
B_COS = 17408
B_SIN = 19584
B_BRI = 21760
B_CFM = 23936
B_UF = 26112
B_TMP1 = 28352
B_TMP2 = 29440
B_EIG = 30528
B_HID = 0
B_COLS = 32768

C_COLS = 16384
CF_CLIP = 4096
CF_RECIP = 5120
CF_ALPHA = 6144
CF_AU = 7168


def _ap(t, off, dims):
    return bass.AP(t, off, dims)


PHASE_LIMIT = 99
DBG_VAR = 0
MF_STOP = 99
TAP = None


def build_nc():
    nc = bass.Bass("TRN2", target_bir_lowering=False, detect_race_conditions=False)

    x_in = nc.dram_tensor("x", [1024, TT], F32, kind="ExternalInput")
    w_inp = nc.dram_tensor("w_inp", [1024, 128], F32, kind="ExternalInput")
    w_out = nc.dram_tensor("w_out", [128, 1024], F32, kind="ExternalInput")
    w_gate = nc.dram_tensor("w_gate", [1024, 1024], F32, kind="ExternalInput")
    w_mlp1 = nc.dram_tensor("w_mlp1", [1024, 4096], F32, kind="ExternalInput")
    w_mlp2 = nc.dram_tensor("w_mlp2", [4096, 1024], F32, kind="ExternalInput")
    w_fg = nc.dram_tensor("w_fg", [1024, 4096], F32, kind="ExternalInput")
    w_fu = nc.dram_tensor("w_fu", [1024, 4096], F32, kind="ExternalInput")
    w_fd = nc.dram_tensor("w_fd", [4096, 1024], F32, kind="ExternalInput")
    wmod = nc.dram_tensor("wmod", [1024, 6144], F32, kind="ExternalInput")
    cpl_in = nc.dram_tensor("cpl", [128, 128], F32, kind="ExternalInput")
    cvec_in = nc.dram_tensor("cvec", [128, 8], F32, kind="ExternalInput")
    vsm_in = nc.dram_tensor("vsmall", [128, 48], F32, kind="ExternalInput")
    b1_in = nc.dram_tensor("b1s", [128, 32], F32, kind="ExternalInput")
    sdec_in = nc.dram_tensor("sdec", [64, 8], F32, kind="ExternalInput")
    freq_in = nc.dram_tensor("freq", [64, 8], F32, kind="ExternalInput")
    hmask_in = nc.dram_tensor("hmask", [128, 1], F32, kind="ExternalInput")
    o_out = nc.dram_tensor("o", [1024, TOK], F32, kind="ExternalOutput")
    wq_s = {k: nc.dram_tensor(f"wq{k}_s", [4096, 1024], BF16) for k in "gud"}

    es = ExitStack()
    sbt = lambda name, shape, dt: es.enter_context(nc.sbuf_tensor(name, shape, dt))

    A = sbt("A", [128, NDC * PITCH], BF16)
    B = sbt("B", [128, B_COLS], BF16)
    C = sbt("C", [128, C_COLS], BF16)
    Dm = sbt("Dm", [128, 1090], F32)
    Dr = sbt("Dr", [128, 1090], F32)
    Wb = sbt("Wb", [128, 3072], BF16)
    Wf = sbt("Wf", [128, 2048], F32)
    winp = sbt("winp", [128, 1024], BF16)
    wout = sbt("wout", [128, 1024], BF16)
    cplm = sbt("cplm", [128, 128], BF16)
    onesb = sbt("onesb", [128, 128], BF16)
    onesf = sbt("onesf", [128, 128], F32)
    ident = sbt("ident", [128, 128], BF16)
    modv = sbt("modv", [128, 384], F32)
    opc = sbt("opc", [128, 16], F32)
    vsm = sbt("vsm", [128, 48], F32)
    b1s = sbt("b1sv", [128, 32], F32)
    cvec = sbt("cvecv", [128, 8], F32)
    silc = sbt("silc", [128, 8], BF16)
    silcr = sbt("silcr", [128, 64], BF16)
    sdec = sbt("sdecv", [64, 8], F32)
    freq = sbt("freqv", [64, 8], F32)
    hmask = sbt("hmaskv", [128, 1], F32)
    mag = sbt("magv", [64, 8], F32)
    thcol = sbt("thcol", [64, 104], F32)
    stat = [sbt(f"stat{i}", [128, 512], F32) for i in range(4)]
    scr = [stat[1], stat[2]]
    scrI = sbt("scrI", [128, 512], I32)
    sigt = [sbt(f"sigt{i}", [128, 512], BF16) for i in range(2)]
    Tsq = sbt("Tsq", [128, 1024], BF16)
    Th = sbt("Th", [128, 4096], BF16)
    canv = sbt("canv", [128, 512], BF16)
    srow = sbt("srow", [128, 512], BF16)
    accw = sbt("accw", [128, 48], F32)
    thw = sbt("thw", [128, 72], F32)
    iot1 = sbt("iot1", [128, 128], F32)
    iot2 = sbt("iot2", [128, 128], F32)

    pb = [es.enter_context(nc.psum_tensor(f"pb{i}", [128, 512], F32)) for i in range(8)]

    sem_names = ["dW", "dG", "T", "V", "S", "G", "OUT"]
    sems = {n: es.enter_context(nc.semaphore("sem_" + n)) for n in sem_names}
    cnt = {n: 0 for n in sem_names}
    streams = {n: [] for n in ["sync", "gpsimd", "vector", "scalar", "tensor"]}

    dead = [False]

    def em(engname, fn, waits=(), inc=None):
        if dead[0]:
            return
        waits = tuple((nm, v) for (nm, v) in waits if v > 0)

        def closure(eng, fn=fn, waits=waits, inc=inc):
            for (nm, v) in waits:
                eng.wait_ge(sems[nm], v)
            ins = fn(eng)
            if inc is not None:
                ins.then_inc(sems[inc], 16 if inc in ("dW", "dG", "OUT") else 1)
        streams[engname].append(closure)

    def bump(n):
        if dead[0]:
            return cnt[n]
        cnt[n] += 16 if n in ("dW", "dG", "OUT") else 1
        return cnt[n]

    def emv(fn, waits=(), barrier=True):
        w = list(waits)
        if barrier:
            w.append(("V", cnt["V"]))
        em("vector", fn, waits=w, inc="V")
        return bump("V")

    bank_free = [("V", 0)] * 8

    def _phase(k):
        if PHASE_LIMIT < k:
            dead[0] = True

    Cf = C.bitcast(F32)
    Bf = B.bitcast(F32)

    def cf_tile(off, w):
        return _ap(Cf, off, [[C_COLS // 2, 128], [1, w]])

    em("gpsimd", lambda e: e.dma_start(
        out=_ap(A, 0, [[NDC * PITCH, 128], [PITCH, NDC], [1, TT]]),
        in_=_ap(x_in, 0, [[TT, 128], [128 * TT, NDC], [1, TT]])), inc="dG")
    dg_x = bump("dG")
    for t, src in [(vsm, vsm_in), (b1s, b1_in), (cvec, cvec_in), (sdec, sdec_in),
                   (freq, freq_in), (hmask, hmask_in), (cplm, cpl_in), (wout, w_out)]:
        em("gpsimd", lambda e, t=t, src=src: e.dma_start(out=t[:], in_=src.ap()), inc="dG")
        bump("dG")
    em("gpsimd", lambda e: e.dma_start(
        out=_ap(winp, 0, [[1024, 128], [128, 8], [1, 128]]),
        in_=_ap(w_inp, 0, [[128, 128], [128 * 128, 8], [1, 128]])), inc="dG")
    dg_small = bump("dG")

    em("gpsimd", lambda e: e.iota(iot1[:], [[1, 128]], channel_multiplier=0,
                                  allow_small_or_imprecise_dtypes=True), inc="G")
    g_i1 = bump("G")
    em("gpsimd", lambda e: e.iota(iot2[:], [[1, 128]], channel_multiplier=1,
                                  allow_small_or_imprecise_dtypes=True), inc="G")
    g_i2 = bump("G")
    em("vector", lambda e: e.memset(onesb[:], 1.0))
    em("vector", lambda e: e.memset(onesf[:], 1.0))
    em("vector", lambda e: e.memset(canv[:], 0.0))
    em("vector", lambda e: e.tensor_sub(iot2[:], iot2[:], iot1[:]),
       waits=[("G", g_i2)], inc="V")
    bump("V")
    v_ident = emv(lambda e: e.tensor_tensor(ident[:], iot1[:], iot2[:], OP.is_equal),
                  waits=[("G", g_i1)])

    em("scalar", lambda e: e.activation(silc[:], cvec[:], AF.Silu),
       waits=[("dG", dg_small)], inc="S")
    s_silc = bump("S")

    for dc in range(NDC):
        em("vector", lambda e, dc=dc: e.tensor_copy(
            silcr[:, 8 * dc:8 * dc + 8], _ap(silc, dc, [[8, 128], [0, 8]])),
            waits=[("S", s_silc)] if dc == 0 else (), inc="V" if dc == NDC - 1 else None)
    v_silcr = bump("V")
    for mc in range(48):
        slot = mc % 2
        em("gpsimd", lambda e, mc=mc, slot=slot: e.dma_start(
            out=_ap(Wb, slot * 1024, [[3072, 128], [128, 8], [1, 128]]),
            in_=_ap(wmod, mc * 128, [[6144, 128], [128 * 6144, 8], [1, 128]])),
            waits=[("T", cnt["T"])] if mc >= 2 else (), inc="dG")
        d_here = bump("dG")
        bi = mc % 2
        bwaits = [("dG", d_here), ("V", v_silcr), bank_free[bi]]
        for dc in range(NDC):
            em("tensor", lambda e, dc=dc, slot=slot, bi=bi: e.matmul(
                pb[bi].ap()[:, 0:8],
                _ap(Wb, slot * 1024 + dc * 128, [[3072, 128], [1, 128]]),
                silcr[:, 8 * dc:8 * dc + 8],
                start=(dc == 0), stop=(dc == NDC - 1)),
                waits=bwaits if dc == 0 else (), inc="T" if dc == NDC - 1 else None)
        t_here = bump("T")
        em("vector", lambda e, mc=mc, bi=bi: e.tensor_copy(modv[:, 8 * mc:8 * mc + 8],
                                                           pb[bi].ap()[:, 0:8]),
           waits=[("T", t_here)], inc="V")
        bank_free[bi] = ("V", bump("V"))

    def mslice(col0):
        return _ap(modv, 8 * col0, [[384, 128], [1, 64]])

    def _floor_gate(col0):
        em("scalar", lambda e, col0=col0: e.activation(stat[0][:, 0:64],
                                                       mslice(col0), AF.Sign),
           waits=[("V", cnt["V"])], inc="S")
        s1 = bump("S")
        em("scalar", lambda e, col0=col0: e.activation(stat[1][:, 0:64],
                                                       mslice(col0), AF.Abs),
           inc="S")
        s2 = bump("S")
        emv(lambda e: e.tensor_scalar_max(stat[1][:, 0:64], stat[1][:, 0:64],
                                          0.25), waits=[("S", s2)])
        return emv(lambda e, col0=col0: e.tensor_mul(mslice(col0),
                                                     stat[0][:, 0:64], stat[1][:, 0:64]),
                   waits=[("S", s1)])
    _floor_gate(16)
    _floor_gate(40)
    emv(lambda e: e.tensor_scalar_add(opc[:, 0:8],
                                      _ap(modv, 64, [[384, 128], [8, 8]]), 1.0))
    v_mod = emv(lambda e: e.tensor_scalar_add(opc[:, 8:16],
                                              _ap(modv, 256, [[384, 128], [8, 8]]), 1.0))

    em("scalar", lambda e: e.activation(mag[:], sdec[:], AF.Sigmoid),
       waits=[("dG", dg_small)], inc="S")
    s_mag = bump("S")
    em("vector", lambda e: e.memset(Dm[:, 0:2], 0.0))
    emv(lambda e: e.tensor_copy(Dm[0:64, 2:1090],
                                _ap(mag, 0, [[8, 64], [0, 1088]])),
        waits=[("S", s_mag)])
    v_dm = emv(lambda e: e.tensor_copy(Dm[64:128, 2:1090], Dm[0:64, 2:1090]))

    def maskap(w):
        return _ap(B, B_TMP1, [[B_COLS, 64], [1, w]])

    def _wrap(ap_get, w):
        emv(lambda e, w=w: e.tensor_single_scalar(maskap(w), ap_get(), PI, OP.is_ge))
        emv(lambda e, w=w: e.scalar_tensor_tensor(ap_get(), maskap(w), -2 * PI,
                                                  ap_get(), OP.mult, OP.add))
        emv(lambda e, w=w: e.tensor_single_scalar(maskap(w), ap_get(), -PI, OP.is_lt))
        emv(lambda e, w=w: e.scalar_tensor_tensor(ap_get(), maskap(w), 2 * PI,
                                                  ap_get(), OP.mult, OP.add))

    emv(lambda e: e.tensor_copy(thcol[:, 0:8], freq[:]), waits=[("dG", dg_small)])
    _wrap(lambda: thcol[:, 0:8], 8)
    for k in range(1, 12):
        emv(lambda e, k=k: e.tensor_scalar_mul(thcol[:, 8 * k:8 * k + 8],
                                               thcol[:, 8 * (k - 1):8 * k], 2.0))
        _wrap(lambda k=k: thcol[:, 8 * k:8 * k + 8], 8)

    def PHf(c0, w):
        return _ap(Bf, B_PH // 2 + c0, [[B_COLS // 2, 64], [1, w]])

    def PH2f(c0, w):
        return _ap(Bf, B_PH2 // 2 + c0, [[B_COLS // 2, 64], [1, w]])

    emv(lambda e: e.memset(PHf(0, 2), 0.0))
    cur = 1
    k = 0
    while cur < 2176:
        w = min(cur, 2176 - cur)
        emv(lambda e, cur=cur, w=w, k=k: e.tensor_scalar(
            PHf(cur, w), PHf(0, w), thcol[0:64, 8 * k:8 * k + 1], None, OP.add))
        _wrap(lambda cur=cur, w=w: PHf(cur, w), w)
        cur += w
        k += 1
    emv(lambda e: e.tensor_scalar_add(PH2f(0, 2176), PHf(0, 2176), PI / 2))
    _wrap(lambda: PH2f(0, 2176), 2176)
    em("vector", lambda e: e.engine_nop(), inc="V")
    v_ph = bump("V")
    _phase(0.5)
    em("scalar", lambda e: e.activation(_ap(B, B_TMP1, [[B_COLS, 64], [1, 1088]]),
                                        PH2f(0, 1088), AF.Sin), waits=[("V", v_ph)])
    em("scalar", lambda e: e.activation(_ap(B, B_TMP2, [[B_COLS, 64], [1, 1088]]),
                                        PH2f(1088, 1088), AF.Sin))
    em("scalar", lambda e: e.activation(_ap(B, B_CFM, [[B_COLS, 64], [1, 2176]]),
                                        PHf(0, 2176), AF.Sin), inc="S")
    s_tab = bump("S")
    emv(lambda e: e.tensor_copy(_ap(B, B_COS, [[B_COLS, 64], [1, 1088]]),
                                _ap(B, B_TMP1, [[B_COLS, 64], [1, 1088]])),
        waits=[("S", s_tab)], barrier=False)
    emv(lambda e: e.tensor_copy(_ap(B, B_COS + 1088, [[B_COLS, 64], [1, 1088]]),
                                _ap(B, B_TMP2, [[B_COLS, 64], [1, 1088]])),
        barrier=False)
    emv(lambda e: e.tensor_copy(_ap(B, B_SIN, [[B_COLS, 64], [1, 2176]]),
                                _ap(B, B_CFM, [[B_COLS, 64], [1, 2176]])),
        barrier=False)
    emv(lambda e: e.tensor_copy(
        bass.AP(B, 64 * B_COLS + B_COS, [[B_COLS, 64], [1, 2176]]),
        _ap(B, B_COS, [[B_COLS, 64], [1, 2176]])))
    v_tab = emv(lambda e: e.tensor_copy(
        bass.AP(B, 64 * B_COLS + B_SIN, [[B_COLS, 64], [1, 2176]]),
        _ap(B, B_SIN, [[B_COLS, 64], [1, 2176]])), barrier=False)

    def ln_stage(src_cols, dst_write, subs, wait_src=None):
        for (off, W) in subs:
            for dc in range(NDC):
                em("tensor", lambda e, dc=dc, off=off, W=W: e.matmul(
                    pb[0].ap()[:, 0:W], onesb[:], src_cols(dc, off, W),
                    start=(dc == 0), stop=(dc == NDC - 1)),
                    waits=([bank_free[0]] + ([wait_src] if wait_src else []))
                    if dc == 0 else (), inc="T" if dc == NDC - 1 else None)
            t_sx = bump("T")
            sq_v = {}
            sq_t = {}
            for dc in range(NDC):
                wts = []
                if wait_src and dc == 0:
                    wts.append(wait_src)
                if dc >= 2:
                    wts.append(("T", sq_t[dc - 2]))
                em("vector", lambda e, dc=dc, off=off, W=W: e.tensor_mul(
                    Tsq[:, (dc % 2) * 512:(dc % 2) * 512 + W], src_cols(dc, off, W),
                    src_cols(dc, off, W)), waits=wts, inc="V")
                sq_v[dc] = bump("V")
                em("tensor", lambda e, dc=dc, W=W: e.matmul(
                    pb[1].ap()[:, 0:W], onesb[:], Tsq[:, (dc % 2) * 512:(dc % 2) * 512 + W],
                    start=(dc == 0), stop=(dc == NDC - 1)),
                    waits=[bank_free[1], ("V", sq_v[dc])] if dc == 0
                    else [("V", sq_v[dc])], inc="T")
                sq_t[dc] = bump("T")
            t_sxx = sq_t[NDC - 1]
            em("vector", lambda e, W=W: e.tensor_scalar_mul(stat[0][:, 0:W],
                                                            pb[0].ap()[:, 0:W], 1.0 / 1024),
               waits=[("T", t_sx)])
            em("vector", lambda e, W=W: e.tensor_scalar_mul(stat[1][:, 0:W],
                                                            pb[1].ap()[:, 0:W], 1.0 / 1024),
               waits=[("T", t_sxx)], inc="V")
            vf = bump("V")
            bank_free[0] = ("V", vf)
            bank_free[1] = ("V", vf)
            emv(lambda e, W=W: e.tensor_mul(stat[2][:, 0:W], stat[0][:, 0:W],
                                            stat[0][:, 0:W]))
            emv(lambda e, W=W: e.tensor_sub(stat[1][:, 0:W], stat[1][:, 0:W],
                                            stat[2][:, 0:W]))
            emv(lambda e, W=W: e.tensor_scalar_add(stat[1][:, 0:W],
                                                   stat[1][:, 0:W], LN_EPS))
            v_alg = emv(lambda e, W=W: e.reciprocal(stat[2][:, 0:W], stat[1][:, 0:W]))
            em("scalar", lambda e, W=W: e.activation(stat[3][:, 0:W], stat[2][:, 0:W],
                                                     AF.Sqrt), waits=[("V", v_alg)],
               inc="S")
            s_rstd = bump("S")
            for dc in range(NDC):
                sl = scr[dc % 2]
                em("vector", lambda e, dc=dc, off=off, W=W, sl=sl: e.tensor_sub(
                    sl[:, 0:W], src_cols(dc, off, W), stat[0][:, 0:W]),
                    waits=[("S", cnt["S"])], inc="V")
                bump("V")
                v_ap = emv(lambda e, W=W, sl=sl: e.tensor_mul(sl[:, 0:W], sl[:, 0:W],
                                                              stat[3][:, 0:W]))
                dst_write(dc, off, W, sl, v_ap)

    def own_ranges(off, W):
        lo, hi = max(off, HALO), min(off + W, TT)
        while lo < hi:
            tau = lo - HALO
            hf = tau // 1024
            end = min(hi, HALO + (hf + 1) * 1024)
            yield (lo - off, hf, tau - hf * 1024, end - lo)
            lo = end

    _phase(1)
    def src_A_sub(off0):
        def f(dc, off, W, off0=off0):
            return A[:, dc * PITCH + off0: dc * PITCH + off0 + W]
        return f

    h_last_s = [0]

    def write_h(dc, off, W, sl, v_ap):
        em("scalar", lambda e, dc=dc, W=W, sl=sl: e.activation(
            Th[:, dc * 512:dc * 512 + W], sl[:, 0:W], AF.Identity,
            bias=modv[:, 8 * dc:8 * dc + 1], scale=opc[:, dc:dc + 1]),
            waits=[("V", v_ap), ("V", v_mod)], inc="S")
        s_h = bump("S")
        h_last_s[0] = s_h

    for (off0, W0) in SUBS_FULL:
        ln_stage(src_A_sub(off0), write_h, [(0, W0)],
                 wait_src=("dG", dg_x))
        s_hsub = h_last_s[0]
        for dc in range(NDC):
            for (rel, hf, tq, w) in own_ranges(off0, W0):
                em("vector", lambda e, dc=dc, rel=rel, hf=hf, tq=tq, w=w: e.tensor_copy(
                    C[:, hf * 8192 + dc * 1024 + tq: hf * 8192 + dc * 1024 + tq + w],
                    Th[:, dc * 512 + rel: dc * 512 + rel + w]),
                    waits=[("S", s_hsub)], inc="V")
                bump("V")

        def src_h(dc, off, W):
            return Th[:, dc * 512: dc * 512 + W]

        def write_xn(dc, off, W, sl, v_ap, off0=off0):
            em("scalar", lambda e, dc=dc, W=W, sl=sl, off0=off0: e.activation(
                B[:, B_XN + dc * PITCH + off0: B_XN + dc * PITCH + off0 + W], sl[:, 0:W],
                AF.Identity, bias=vsm[:, 8 + dc:9 + dc], scale=vsm[:, dc:dc + 1]),
                waits=[("V", v_ap)], inc="S")
            bump("S")

        ln_stage(src_h, write_xn, [(0, W0)], wait_src=("S", s_hsub))
    s_xn = cnt["S"]
    v_hC = cnt["V"]

    _phase(2)
    for (off0, W0) in SUBS_FULL:
        for dc in range(NDC):
            em("tensor", lambda e, dc=dc, off0=off0, W0=W0: e.matmul(
                pb[2].ap()[:, 0:W0],
                _ap(winp, dc * 128, [[1024, 128], [1, 128]]),
                B[:, B_XN + dc * PITCH + off0: B_XN + dc * PITCH + off0 + W0],
                start=(dc == 0), stop=(dc == NDC - 1)),
                waits=[bank_free[2], ("S", s_xn), ("dG", dg_small)] if dc == 0 else (),
                inc="T" if dc == NDC - 1 else None)
        t_b = bump("T")
        em("vector", lambda e, off0=off0, W0=W0: e.tensor_copy(
            B[:, B_BRI + off0: B_BRI + off0 + W0], pb[2].ap()[:, 0:W0]),
            waits=[("T", t_b)], inc="V")
        bank_free[2] = ("V", bump("V"))
    v_bmask = emv(lambda e: e.tensor_scalar(B[:, B_BRI:B_BRI + HALO],
                                            B[:, B_BRI:B_BRI + HALO],
                                            hmask[:, 0:1], None, OP.mult))

    def b64(col, w, hi=False):
        return bass.AP(B, (64 * B_COLS if hi else 0) + col, [[B_COLS, 64], [1, w]])

    def uf(col, w):
        return _ap(Bf, B_UF // 2 + col, [[B_COLS // 2, 128], [1, w]])

    for hf in range(2):
        c0 = hf * 1088
        emv(lambda e, c0=c0: e.tensor_mul(
            _ap(B, B_TMP1, [[B_COLS, 128], [1, 1088]]),
            B[:, B_BRI + c0:B_BRI + c0 + 1088], B[:, B_COS + c0:B_COS + c0 + 1088]),
            waits=[("V", v_bmask), ("V", v_tab)])
        emv(lambda e, c0=c0: e.tensor_mul(
            _ap(B, B_TMP2, [[B_COLS, 128], [1, 1088]]),
            B[:, B_BRI + c0:B_BRI + c0 + 1088], B[:, B_SIN + c0:B_SIN + c0 + 1088]))
        emv(lambda e: e.tensor_copy(b64(B_EIG, 1088), b64(B_TMP2, 1088, hi=True)))
        emv(lambda e: e.tensor_add(Dr[0:64, 2:1090], b64(B_TMP1, 1088),
                                   b64(B_EIG, 1088)))
        emv(lambda e: e.tensor_copy(b64(B_EIG, 1088), b64(B_TMP1, 1088, hi=True)))
        emv(lambda e: e.tensor_tensor(
            bass.AP(Dr, 64 * 1090 + 2, [[1090, 64], [1, 1088]]),
            b64(B_EIG, 1088), b64(B_TMP2, 1088), OP.subtract))
        if hf == 0:
            emv(lambda e: e.tensor_tensor_scan(
                uf(0, 1088), Dm[:, 2:1090], Dr[:, 2:1090], 0.0, OP.mult, OP.add),
                waits=[("V", v_dm)])
        else:
            emv(lambda e: e.tensor_copy(Dr[:, 0:2], uf(1086, 2)))
            emv(lambda e: e.tensor_tensor_scan(
                uf(0, 1090), Dm[:, 0:1090], Dr[:, 0:1090], 0.0, OP.mult, OP.add))
        ush = 2 if hf == 1 else 0
        emv(lambda e, c0=c0, ush=ush: e.tensor_mul(
            _ap(B, B_TMP1, [[B_COLS, 128], [1, 1088]]), uf(ush, 1088),
            B[:, B_COS + c0:B_COS + c0 + 1088]))
        emv(lambda e, c0=c0, ush=ush: e.tensor_mul(
            _ap(B, B_TMP2, [[B_COLS, 128], [1, 1088]]), uf(ush, 1088),
            B[:, B_SIN + c0:B_SIN + c0 + 1088]))
        emv(lambda e: e.tensor_copy(b64(B_EIG, 1088), b64(B_TMP2, 1088, hi=True)))
        emv(lambda e, c0=c0: e.tensor_tensor(
            b64(B_CFM + c0, 1088), b64(B_TMP1, 1088), b64(B_EIG, 1088), OP.subtract))
        emv(lambda e: e.tensor_copy(b64(B_EIG, 1088), b64(B_TMP1, 1088, hi=True)))
        emv(lambda e, c0=c0: e.tensor_tensor(
            bass.AP(B, 64 * B_COLS + B_CFM + c0, [[B_COLS, 64], [1, 1088]]),
            b64(B_TMP2, 1088), b64(B_EIG, 1088), OP.add))
    v_conv = cnt["V"]

    for (soff, W) in SUBS_OWN:
        em("tensor", lambda e, soff=soff, W=W: e.matmul(
            pb[3].ap()[:, 0:W], cplm[:], B[:, B_CFM + HALO + soff: B_CFM + HALO + soff + W],
            start=True, stop=True),
            waits=[bank_free[3], ("V", v_conv), ("dG", dg_small)], inc="T")
        t_e = bump("T")
        em("vector", lambda e, soff=soff, W=W: e.tensor_copy(
            B[:, B_EIG + soff: B_EIG + soff + W], pb[3].ap()[:, 0:W]),
            waits=[("T", t_e)], inc="V")
        bank_free[3] = ("V", bump("V"))
    v_eig = cnt["V"]

    _phase(3)
    sig_free = [("V", 0), ("V", 0)]
    for ec in range(NDC):
        slot = ec % 2
        em("gpsimd", lambda e, ec=ec, slot=slot: e.dma_start(
            out=_ap(Wb, slot * 1024, [[3072, 128], [128, 8], [1, 128]]),
            in_=_ap(w_gate, ec * 128, [[1024, 128], [128 * 1024, 8], [1, 128]])),
            waits=[("T", cnt["T"])] if ec >= 2 else (), inc="dG")
        d_g = bump("dG")
        for si, (soff, W) in enumerate(SUBS_OWN):
            step = ec * len(SUBS_OWN) + si
            bgi = 4 + step % 2
            for dc in range(NDC):
                em("tensor", lambda e, dc=dc, soff=soff, W=W, slot=slot, bgi=bgi: e.matmul(
                    pb[bgi].ap()[:, 0:W],
                    _ap(Wb, slot * 1024 + dc * 128, [[3072, 128], [1, 128]]),
                    B[:, B_XN + dc * PITCH + HALO + soff:
                      B_XN + dc * PITCH + HALO + soff + W],
                    start=(dc == 0), stop=(dc == NDC - 1)),
                    waits=[("dG", d_g), bank_free[bgi]] if dc == 0 else (),
                    inc="T" if dc == NDC - 1 else None)
            t_g = bump("T")
            ssl = step % 2
            em("scalar", lambda e, W=W, bgi=bgi, ec=ec, ssl=ssl: e.activation(
                sigt[ssl][:, 0:W], pb[bgi].ap()[:, 0:W], AF.Sigmoid,
                bias=vsm[:, 32 + ec:33 + ec]),
                waits=[("T", t_g), sig_free[ssl]], inc="S")
            s_sig = bump("S")
            bank_free[bgi] = ("S", s_sig)
            em("tensor", lambda e, ec=ec, soff=soff, W=W: e.matmul(
                pb[6].ap()[:, 0:W],
                _ap(wout, ec * 128, [[1024, 128], [1, 128]]),
                B[:, B_EIG + soff: B_EIG + soff + W], start=True, stop=True),
                waits=[bank_free[6], ("V", v_eig)], inc="T")
            t_o = bump("T")
            em("vector", lambda e, W=W, ssl=ssl: e.tensor_mul(
                scr[0][:, 0:W], sigt[ssl][:, 0:W], pb[6].ap()[:, 0:W]),
                waits=[("T", t_o), ("S", s_sig), ("V", v_hC)], inc="V")
            v_mu = bump("V")
            bank_free[6] = ("V", v_mu)
            sig_free[ssl] = ("V", v_mu)
            hf2, tq2 = soff // 1024, soff % 1024
            base = hf2 * 8192 + ec * 1024 + tq2
            emv(lambda e, base=base, W=W: e.tensor_add(
                C[:, base:base + W], C[:, base:base + W], scr[0][:, 0:W]))
    v_xi = cnt["V"]

    for dc in range(NDC):
        for hf in range(2):
            emv(lambda e, dc=dc, hf=hf: e.scalar_tensor_tensor(
                A[:, dc * PITCH + HALO + hf * 1024: dc * PITCH + HALO + hf * 1024 + 1024],
                C[:, hf * 8192 + dc * 1024: hf * 8192 + dc * 1024 + 1024],
                modv[:, 8 * (16 + dc):8 * (16 + dc) + 1],
                A[:, dc * PITCH + HALO + hf * 1024: dc * PITCH + HALO + hf * 1024 + 1024],
                OP.mult, OP.add), barrier=(dc == 0 and hf == 0))
    v_x1a = cnt["V"]

    _phase(4)
    for hf in range(2):
        def src_xi(dc, off, W, hf=hf):
            return C[:, hf * 8192 + dc * 1024 + off: hf * 8192 + dc * 1024 + off + W]

        def write_x2(dc, off, W, sl, v_ap, hf=hf):
            em("scalar", lambda e, dc=dc, off=off, W=W, sl=sl, hf=hf: e.activation(
                C[:, hf * 8192 + dc * 1024 + off: hf * 8192 + dc * 1024 + off + W],
                sl[:, 0:W], AF.Identity, bias=vsm[:, 24 + dc:25 + dc],
                scale=vsm[:, 16 + dc:17 + dc]), waits=[("V", v_ap)], inc="S")
            bump("S")

        ln_stage(src_xi, write_x2, [(0, 512), (512, 512)], wait_src=("V", v_x1a))
        s_x2 = cnt["S"]

        for mc in range(NMC):
            slot = mc % 2
            em("gpsimd", lambda e, mc=mc, slot=slot: e.dma_start(
                out=_ap(Wb, slot * 1024, [[3072, 128], [128, 8], [1, 128]]),
                in_=_ap(w_mlp1, mc * 128, [[4096, 128], [128 * 4096, 8], [1, 128]])),
                waits=[("T", cnt["T"])], inc="dG")
            d_m = bump("dG")
            for q in range(2):
                bi = (mc * 2 + q) % 4
                for dc in range(NDC):
                    em("tensor", lambda e, dc=dc, q=q, slot=slot, bi=bi, hf=hf: e.matmul(
                        pb[bi].ap()[:, 0:512],
                        _ap(Wb, slot * 1024 + dc * 128, [[3072, 128], [1, 128]]),
                        C[:, hf * 8192 + dc * 1024 + q * 512:
                          hf * 8192 + dc * 1024 + q * 512 + 512],
                        start=(dc == 0), stop=(dc == NDC - 1)),
                        waits=[("dG", d_m), bank_free[bi], ("S", s_x2)]
                        if dc == 0 else (), inc="T" if dc == NDC - 1 else None)
                t_m = bump("T")
                em("scalar", lambda e, mc=mc, q=q, bi=bi: e.activation(
                    B[:, B_HID + mc * 1024 + q * 512: B_HID + mc * 1024 + q * 512 + 512],
                    pb[bi].ap()[:, 0:512], AF.Silu, bias=b1s[:, mc:mc + 1]),
                    waits=[("T", t_m)], inc="S")
                bank_free[bi] = ("S", bump("S"))
        s_hid = cnt["S"]

        for q in range(2):
            t_prev = {}
            for mc in range(NMC):
                slot = mc % 2
                wts = [("T", t_prev[mc - 2])] if mc >= 2 else []
                em("gpsimd", lambda e, mc=mc, slot=slot: e.dma_start(
                    out=_ap(Wb, slot * 1024, [[3072, 128], [128, 8], [1, 128]]),
                    in_=_ap(w_mlp2, mc * 128 * 1024, [[1024, 128], [128, 8], [1, 128]])),
                    waits=wts + ([("T", cnt["T"])] if mc < 2 else []), inc="dG")
                d_m2 = bump("dG")
                for dc in range(NDC):
                    em("tensor", lambda e, dc=dc, mc=mc, q=q, slot=slot: e.matmul(
                        pb[dc].ap()[:, 0:512],
                        _ap(Wb, slot * 1024 + dc * 128, [[3072, 128], [1, 128]]),
                        B[:, B_HID + mc * 1024 + q * 512: B_HID + mc * 1024 + q * 512 + 512],
                        start=(mc == 0), stop=(mc == NMC - 1)),
                        waits=([("dG", d_m2), ("S", s_hid)] +
                               ([bank_free[dc]] if mc == 0 else []))
                        if dc == 0 else ([bank_free[dc]] if mc == 0 else ()),
                        inc="T" if dc == NDC - 1 else None)
                t_prev[mc] = bump("T")
            t_d = t_prev[NMC - 1]
            for dc in range(NDC):
                em("scalar", lambda e, dc=dc: e.activation(
                    scr[dc % 2][:, 0:512], pb[dc].ap()[:, 0:512], AF.Identity,
                    bias=vsm[:, 40 + dc:41 + dc]),
                    waits=[("T", t_d), ("V", cnt["V"])], inc="S")
                s_e = bump("S")
                em("vector", lambda e, dc=dc, q=q, hf=hf: e.scalar_tensor_tensor(
                    A[:, dc * PITCH + HALO + hf * 1024 + q * 512:
                      dc * PITCH + HALO + hf * 1024 + q * 512 + 512],
                    scr[dc % 2][:, 0:512], modv[:, 8 * (16 + dc):8 * (16 + dc) + 1],
                    A[:, dc * PITCH + HALO + hf * 1024 + q * 512:
                      dc * PITCH + HALO + hf * 1024 + q * 512 + 512],
                    OP.mult, OP.add), waits=[("S", s_e)], inc="V")
                bank_free[dc] = ("V", bump("V"))
    v_x1 = cnt["V"]

    _phase(5)
    wsrc = [("g", w_fg, False), ("u", w_fu, False), ("d", w_fd, True)]

    def wslice_ap(wdram, mc, is_d):
        if is_d:
            return _ap(wdram, mc * 128 * 1024, [[1024, 128], [1, 1024]])
        return _ap(wdram, mc * 128, [[4096, 128], [128 * 4096, 8], [1, 128]])

    for col, (k, wdram, is_d) in enumerate(wsrc):
        for mc in range(NMC):
            slot = mc % 2
            em("sync", lambda e, mc=mc, slot=slot, wdram=wdram, is_d=is_d: e.dma_start(
                out=_ap(Wf, slot * 1024, [[2048, 128], [128, 8], [1, 128]])
                if not is_d else Wf[:, slot * 1024:(slot + 1) * 1024],
                in_=wslice_ap(wdram, mc, is_d)),
                waits=[("S", cnt["S"])] if mc >= 2 else (), inc="dW")
            d_w = bump("dW")
            em("scalar", lambda e, slot=slot, mc=mc: e.activation(
                _ap(Wb, 0, [[3072, 128], [1, 1024]]), Wf[:, slot * 1024:(slot + 1) * 1024],
                AF.Abs, accum_out=accw[:, mc:mc + 1]), waits=[("dW", d_w)], inc="S")
            bump("S")
        s_abs = cnt["S"]
        em("vector", lambda e: e.tensor_reduce(accw[:, 32:33], accw[:, 0:32], AX.X, OP.add),
           waits=[("S", s_abs)], inc="V")
        bump("V")
        v_red = emv(lambda e: e.tensor_copy(accw[:, 40:48],
                                            _ap(accw, 32, [[48, 128], [0, 8]])))
        em("tensor", lambda e: e.matmul(pb[7].ap()[:, 0:8], onesf[:], accw[:, 40:48],
                                        start=True, stop=True),
           waits=[("V", v_red), bank_free[7]], inc="T")
        t_th = bump("T")
        em("vector", lambda e, col=col: e.tensor_scalar_mul(
            thw[:, 8 * col:8 * col + 8], pb[7].ap()[:, 0:8], 0.5 / (4096.0 * 1024.0)),
            waits=[("T", t_th)], inc="V")
        bank_free[7] = ("V", bump("V"))
        emv(lambda e, col=col: e.tensor_scalar_mul(
            thw[:, 8 * (3 + col):8 * (3 + col) + 8], thw[:, 8 * col:8 * col + 8],
            1.0 / 127.0))
        em("vector", lambda e, col=col: e.tensor_scalar_mul(
            thw[:, 8 * (6 + col):8 * (6 + col) + 8], thw[:, 8 * col:8 * col + 8], -1.0),
            inc="V")
        v_th = bump("V")
        for mc in range(NMC):
            slot = mc % 2
            em("sync", lambda e, mc=mc, slot=slot, wdram=wdram, is_d=is_d: e.dma_start(
                out=_ap(Wf, slot * 1024, [[2048, 128], [128, 8], [1, 128]])
                if not is_d else Wf[:, slot * 1024:(slot + 1) * 1024],
                in_=wslice_ap(wdram, mc, is_d)),
                waits=[("dG", cnt["dG"])] if mc >= 2 else [("V", v_th)], inc="dW")
            d_w = bump("dW")
            em("scalar", lambda e, slot=slot, col=col: e.activation(
                _ap(Wb, 0, [[3072, 128], [1, 1024]]), Wf[:, slot * 1024:(slot + 1) * 1024],
                AF.Sign, bias=thw[:, 8 * (6 + col):8 * (6 + col) + 1]), waits=[("dW", d_w)], inc="S")
            bump("S")
            em("scalar", lambda e, slot=slot, col=col: e.activation(
                _ap(Wb, 1024, [[3072, 128], [1, 1024]]), Wf[:, slot * 1024:(slot + 1) * 1024],
                AF.Sign, bias=thw[:, 8 * col:8 * col + 1]), inc="S")
            s_q2 = bump("S")
            em("vector", lambda e: e.tensor_add(
                _ap(Wb, 2048, [[3072, 128], [1, 1024]]),
                _ap(Wb, 0, [[3072, 128], [1, 1024]]),
                _ap(Wb, 1024, [[3072, 128], [1, 1024]])), waits=[("S", s_q2)], inc="V")
            v_q = bump("V")
            em("gpsimd", lambda e, mc=mc, k=k: e.dma_start(
                out=_ap(wq_s[k], mc * 128 * 1024, [[1024, 128], [1, 1024]]),
                in_=_ap(Wb, 2048, [[3072, 128], [1, 1024]])), waits=[("V", v_q)], inc="dG")
            bump("dG")
    dg_wq = cnt["dG"]

    pb7b = pb[7].bitcast(BF16)

    def pb7bap(c0, w):
        return bass.AP(pb7b, c0, [[1024, 128], [1, w]])

    def emit_maxfold(pos_src, W, out_f32_off, waits0):
        ng = W // 128
        mf = MF_STOP
        em("vector", lambda e, W=W: e.tensor_copy(Tsq[:, 0:W], pos_src()),
           waits=waits0, inc="V")
        v_pos = bump("V")
        for g in range(ng):
            em("tensor", lambda e, g=g: e.transpose(
                pb7bap(g * 128, 128), Tsq[:, g * 128:(g + 1) * 128], ident[:]),
                waits=[("V", v_pos), bank_free[7], ("V", v_ident)] if g == 0 else (),
                inc="T" if g == ng - 1 else None)
        t_tr = bump("T")
        em("vector", lambda e, W=W: e.tensor_copy(Tsq[:, 512:512 + W], pb7bap(0, W)),
           waits=[("T", t_tr)], inc="V")
        bump("V")
        if mf < 2:
            return
        for g in range(ng):
            em("vector", lambda e, g=g: e.tensor_reduce(
                canv[:, 128 * g:128 * g + 1], Tsq[:, 512 + g * 128:512 + (g + 1) * 128],
                AX.X, OP.max), waits=[("V", cnt["V"])] if g == 0 else (),
                inc="V" if g == ng - 1 else None)
        v_c = bump("V")
        bank_free[7] = ("V", v_c)
        if mf < 3:
            return
        for g in range(ng):
            em("tensor", lambda e, g=g: e.transpose(
                pb7bap(g * 128, 128), canv[:, g * 128:(g + 1) * 128], ident[:]),
                waits=[("V", v_c)] if g == 0 else (), inc="T" if g == ng - 1 else None)
        t_tr2 = bump("T")
        em("vector", lambda e, W=W: e.tensor_copy(srow[:, 0:W], pb7bap(0, W)),
           waits=[("T", t_tr2)], inc="V")
        v_sr = bump("V")
        bank_free[7] = ("V", v_sr)
        if mf < 4:
            return
        for g in range(ng):
            em("tensor", lambda e, g=g: e.matmul(
                pb[6].ap()[:, g * 128:(g + 1) * 128], onesb[0:1, :],
                bass.AP(srow, 128 * g, [[512, 1], [1, 128]]),
                start=True, stop=True),
                waits=[("V", v_sr), bank_free[6]] if g == 0 else (),
                inc="T" if g == ng - 1 else None)
        t_rep = bump("T")
        em("vector", lambda e, W=W, off=out_f32_off: e.tensor_scalar_max(
            cf_tile(off, W), pb[6].ap()[:, 0:W], 1e-5), waits=[("T", t_rep)], inc="V")
        bank_free[6] = ("V", bump("V"))

    _phase(6)
    for hf in range(2):
        for (off, W) in [(0, 512), (512, 512)]:
            def src_X2(dc, o, W2, hf=hf, off=off):
                return A[:, dc * PITCH + HALO + hf * 1024 + off:
                         dc * PITCH + HALO + hf * 1024 + off + W2]

            def write_h2(dc, o, W2, sl, v_ap):
                em("scalar", lambda e, dc=dc, W2=W2, sl=sl: e.activation(
                    Th[:, dc * 512:dc * 512 + W2], sl[:, 0:W2], AF.Identity,
                    bias=modv[:, 8 * (24 + dc):8 * (24 + dc) + 1], scale=opc[:, 8 + dc:9 + dc]),
                    waits=[("V", v_ap)], inc="S")
                bump("S")

            ln_stage(src_X2, write_h2, [(0, W)], wait_src=("V", v_x1))
            s_h2 = cnt["S"]
            _phase(6.1)
            if DBG_VAR == 1:
                em("vector", lambda e, W=W: e.tensor_reduce(
                    stat[1][:, 0:W], _ap(Th, 0, [[4096, 128], [1, W], [512, 8]]),
                    AX.X, OP.max, apply_absolute_value=False),
                    waits=[("S", s_h2)], inc="V")
            elif DBG_VAR == 2:
                em("vector", lambda e, W=W: e.tensor_reduce(
                    stat[1][:, 0:1], Th[:, 0:4096],
                    AX.X, OP.max, apply_absolute_value=True),
                    waits=[("S", s_h2)], inc="V")
            elif DBG_VAR == 3:
                em("vector", lambda e, W=W: e.tensor_copy(stat[1][:, 0:W],
                                                          Th[:, 0:W]),
                    waits=[("S", s_h2)], inc="V")
            else:
                em("vector", lambda e, W=W: e.tensor_reduce(
                    stat[1][:, 0:W], _ap(Th, 0, [[4096, 128], [1, W], [512, 8]]),
                    AX.X, OP.max, apply_absolute_value=True),
                    waits=[("S", s_h2)], inc="V")
            v_pos = bump("V")
            _phase(6.2)
            emit_maxfold(lambda W=W: stat[1][:, 0:W], W, CF_CLIP + off, [("V", v_pos)])
            _phase(6.3)
            v_rec = emv(lambda e, off=off: e.reciprocal(
                cf_tile(CF_RECIP + off, 512), cf_tile(CF_CLIP + off, 512)))
            em("vector", lambda e, off=off: e.tensor_scalar(
                cf_tile(CF_ALPHA + off, 512), cf_tile(CF_CLIP + off, 512),
                thw[:, 24:25], None, OP.mult))
            em("vector", lambda e, off=off: e.tensor_scalar(
                cf_tile(CF_AU + off, 512), cf_tile(CF_CLIP + off, 512),
                thw[:, 32:33], None, OP.mult), inc="V")
            bump("V")
            _phase(6.4)
            for dc in range(NDC):
                em("vector", lambda e, dc=dc, off=off, W=W: e.scalar_tensor_tensor(
                    scrI[:, 0:W], Th[:, dc * 512:dc * 512 + W], 127.0,
                    cf_tile(CF_RECIP + off, W), OP.mult, OP.mult), inc="V")
                bump("V")
                emv(lambda e, dc=dc, off=off, W=W: e.tensor_copy(
                    C[:, dc * 1024 + off: dc * 1024 + off + W], scrI[:, 0:W]))
        v_q1 = cnt["V"]
        _phase(6.5)

        for mc in range(NMC):
            em("sync", lambda e, mc=mc: e.dma_start(
                out=_ap(Wb, 0, [[3072, 128], [1, 1024]]),
                in_=_ap(wq_s["g"], mc * 128 * 1024, [[1024, 128], [1, 1024]])),
                waits=[("dG", dg_wq)] + ([("T", cnt["T"])] if mc >= 1 else []), inc="dW")
            bump("dW")
            em("sync", lambda e, mc=mc: e.dma_start(
                out=_ap(Wb, 1024, [[3072, 128], [1, 1024]]),
                in_=_ap(wq_s["u"], mc * 128 * 1024, [[1024, 128], [1, 1024]])), inc="dW")
            d_u = bump("dW")
            for q in range(2):
                for dc in range(NDC):
                    em("tensor", lambda e, dc=dc, q=q: e.matmul(
                        pb[q].ap()[:, 0:512],
                        _ap(Wb, dc * 128, [[3072, 128], [1, 128]]),
                        C[:, dc * 1024 + q * 512: dc * 1024 + q * 512 + 512],
                        start=(dc == 0), stop=(dc == NDC - 1)),
                        waits=[("dW", d_u), bank_free[q], ("V", v_q1)] if dc == 0 else (),
                        inc="T" if dc == NDC - 1 else None)
                t_g = bump("T")
                for dc in range(NDC):
                    em("tensor", lambda e, dc=dc, q=q: e.matmul(
                        pb[2 + q].ap()[:, 0:512],
                        _ap(Wb, 1024 + dc * 128, [[3072, 128], [1, 128]]),
                        C[:, dc * 1024 + q * 512: dc * 1024 + q * 512 + 512],
                        start=(dc == 0), stop=(dc == NDC - 1)),
                        waits=[bank_free[2 + q]] if dc == 0 else (),
                        inc="T" if dc == NDC - 1 else None)
                t_u = bump("T")
                em("vector", lambda e, q=q: e.tensor_mul(
                    scr[q][:, 0:512], pb[q].ap()[:, 0:512],
                    cf_tile(CF_ALPHA + q * 512, 512)), waits=[("T", t_g)], inc="V")
                bank_free[q] = ("V", bump("V"))
                em("scalar", lambda e, q=q: e.activation(
                    sigt[q][:, 0:512], scr[q][:, 0:512], AF.Silu),
                    waits=[("V", cnt["V"])], inc="S")
                s_sg = bump("S")
                em("vector", lambda e, mc=mc, q=q: e.tensor_mul(
                    B[:, B_HID + mc * 1024 + q * 512: B_HID + mc * 1024 + q * 512 + 512],
                    sigt[q][:, 0:512], pb[2 + q].ap()[:, 0:512]),
                    waits=[("T", t_u), ("S", s_sg)], inc="V")
                bank_free[2 + q] = ("V", bump("V"))
        v_z = cnt["V"]
        _phase(6.6)

        for q in range(2):
            em("vector", lambda e, q=q: e.tensor_reduce(
                stat[1][:, 0:512],
                _ap(B, B_HID + q * 512, [[B_COLS, 128], [1, 512], [1024, 32]]),
                AX.X, OP.max, apply_absolute_value=True), waits=[("V", v_z)], inc="V")
            v_pos = bump("V")
            emit_maxfold(lambda: stat[1][:, 0:512], 512, CF_CLIP + q * 512, [("V", v_pos)])
            emv(lambda e, q=q: e.reciprocal(
                cf_tile(CF_RECIP + q * 512, 512), cf_tile(CF_CLIP + q * 512, 512)))
            em("vector", lambda e, q=q: e.scalar_tensor_tensor(
                cf_tile(CF_ALPHA + q * 512, 512), cf_tile(CF_CLIP + q * 512, 512),
                thw[:, 40:41], cf_tile(CF_AU + q * 512, 512), OP.mult, OP.mult), inc="V")
            bump("V")
        v_tiles = cnt["V"]
        for mc in range(NMC):
            for q in range(2):
                em("vector", lambda e, mc=mc, q=q: e.scalar_tensor_tensor(
                    scrI[:, 0:512],
                    B[:, B_HID + mc * 1024 + q * 512: B_HID + mc * 1024 + q * 512 + 512],
                    127.0, cf_tile(CF_RECIP + q * 512, 512), OP.mult, OP.mult), inc="V")
                bump("V")
                emv(lambda e, mc=mc, q=q: e.tensor_copy(
                    B[:, B_HID + mc * 1024 + q * 512: B_HID + mc * 1024 + q * 512 + 512],
                    scrI[:, 0:512]))
        v_q2 = cnt["V"]
        _phase(6.7)

        for q in range(2):
            t_prev = {}
            for mc in range(NMC):
                slot = mc % 2
                wts = [("T", t_prev[mc - 2])] if mc >= 2 else [("T", cnt["T"])]
                em("sync", lambda e, mc=mc, slot=slot: e.dma_start(
                    out=_ap(Wb, slot * 1024, [[3072, 128], [128, 8], [1, 128]]),
                    in_=_ap(wq_s["d"], mc * 128 * 1024,
                            [[1024, 128], [128, 8], [1, 128]])),
                    waits=wts, inc="dW")
                d_d = bump("dW")
                for dc in range(NDC):
                    em("tensor", lambda e, dc=dc, mc=mc, q=q, slot=slot: e.matmul(
                        pb[dc].ap()[:, 0:512],
                        _ap(Wb, slot * 1024 + dc * 128, [[3072, 128], [1, 128]]),
                        B[:, B_HID + mc * 1024 + q * 512: B_HID + mc * 1024 + q * 512 + 512],
                        start=(mc == 0), stop=(mc == NMC - 1)),
                        waits=([("dW", d_d), ("V", v_q2)] +
                               ([bank_free[dc]] if mc == 0 else []))
                        if dc == 0 else ([bank_free[dc]] if mc == 0 else ()),
                        inc="T" if dc == NDC - 1 else None)
                t_prev[mc] = bump("T")
            t_d = t_prev[NMC - 1]
            for dc in range(NDC):
                em("vector", lambda e, dc=dc, q=q: e.tensor_mul(
                    scr[dc % 2][:, 0:512], pb[dc].ap()[:, 0:512],
                    cf_tile(CF_ALPHA + q * 512, 512)), waits=[("T", t_d)], inc="V")
                bump("V")
                bank_free[dc] = ("V", emv(lambda e, dc=dc, q=q, hf=hf: e.scalar_tensor_tensor(
                    A[:, dc * PITCH + HALO + hf * 1024 + q * 512:
                      dc * PITCH + HALO + hf * 1024 + q * 512 + 512],
                    scr[dc % 2][:, 0:512], modv[:, 8 * (40 + dc):8 * (40 + dc) + 1],
                    A[:, dc * PITCH + HALO + hf * 1024 + q * 512:
                      dc * PITCH + HALO + hf * 1024 + q * 512 + 512],
                    OP.mult, OP.add)))
    v_done = cnt["V"]

    dead[0] = False
    if TAP is None:
        src_ap = _ap(A, HALO, [[NDC * PITCH, 128], [PITCH, NDC], [1, TOK]])
    else:
        arena, col0 = TAP
        t_ = {"B": B, "C": C, "A": A, "Bf": Bf, "Cf": Cf}[arena]
        pitch = {"B": B_COLS, "C": C_COLS, "A": NDC * PITCH,
                 "Bf": B_COLS // 2, "Cf": C_COLS // 2}[arena]
        src_ap = _ap(t_, col0, [[pitch, 128], [0, NDC], [1, TOK]])
    em("gpsimd", lambda e: e.dma_start(
        out=_ap(o_out, 0, [[TOK, 128], [128 * TOK, NDC], [1, TOK]]),
        in_=src_ap), waits=[("V", v_done)], inc="OUT")
    n_out = bump("OUT")
    em("gpsimd", lambda e, n=n_out: e.wait_ge(sems["OUT"], n))

    with es:
        with nc.Block() as blk:
            @blk.sync
            def _(eng):
                for fn in streams["sync"]:
                    fn(eng)

            @blk.gpsimd
            def _(eng):
                for fn in streams["gpsimd"]:
                    fn(eng)

            @blk.vector
            def _(eng):
                for fn in streams["vector"]:
                    fn(eng)

            @blk.scalar
            def _(eng):
                for fn in streams["scalar"]:
                    fn(eng)

            @blk.tensor
            def _(eng):
                for fn in streams["tensor"]:
                    fn(eng)
    return nc


def _prep_core(inputs, core):
    b, half = core // 2, core % 2
    x = np.asarray(inputs["L0_attn_x"], np.float32)
    xs = np.zeros((TT, 1024), np.float32)
    lo = half * TOK - HALO
    src_lo = max(lo, 0)
    xs[src_lo - lo:, :] = x[b, src_lo: half * TOK + TOK, :]
    ii = {"x": np.ascontiguousarray(xs.T)}
    for name, key in [("w_inp", "L0_attn_in_proj_w"), ("w_out", "L0_attn_out_proj_w"),
                      ("w_gate", "L0_attn_gate_w"), ("w_mlp1", "L0_attn_mlp_w1"),
                      ("w_mlp2", "L0_attn_mlp_w2"), ("w_fg", "L0_ffn_w_gate"),
                      ("w_fu", "L0_ffn_w_up"), ("w_fd", "L0_ffn_w_down")]:
        ii[name] = np.ascontiguousarray(np.asarray(inputs[key], np.float32).T)
    ii["wmod"] = np.ascontiguousarray(np.asarray(inputs["L0_wmod"], np.float32))
    cpl = np.asarray(inputs["L0_attn_coupling"], np.float32)
    cm = np.zeros((128, 128), np.float32)
    for h in range(4):
        cm[h * 16:(h + 1) * 16, h * 16:(h + 1) * 16] = cpl[h].T
        cm[64 + h * 16:64 + (h + 1) * 16, 64 + h * 16:64 + (h + 1) * 16] = cpl[h].T
    ii["cpl"] = cm
    ii["cvec"] = np.ascontiguousarray(
        np.asarray(inputs["L0_c"], np.float32)[b].reshape(8, 128).T)
    vs = np.zeros((128, 48), np.float32)
    for j, k in enumerate(["L0_attn_n1_g", "L0_attn_n1_b", "L0_attn_n2_g", "L0_attn_n2_b",
                           "L0_attn_gate_b", "L0_attn_mlp_b2"]):
        vs[:, 8 * j:8 * (j + 1)] = np.asarray(inputs[k], np.float32).reshape(8, 128).T
    ii["vsmall"] = vs
    ii["b1s"] = np.ascontiguousarray(
        np.asarray(inputs["L0_attn_mlp_b1"], np.float32).reshape(32, 128).T)
    ii["sdec"] = np.ascontiguousarray(np.broadcast_to(
        np.asarray(inputs["L0_attn_log_decay"], np.float32).reshape(64, 1), (64, 8)))
    ii["freq"] = np.ascontiguousarray(np.broadcast_to(
        np.asarray(inputs["L0_attn_frequency"], np.float32).reshape(64, 1), (64, 8)))
    ii["hmask"] = np.full((128, 1), 1.0 if half == 1 else 0.0, np.float32)
    return ii


def kernel(**inputs):
    nc = build_nc()
    in_maps = [_prep_core(inputs, c) for c in range(8)]
    res = run_bass_kernel_spmd(nc, in_maps, core_ids=list(range(8)))
    out = np.zeros((4, 4096, 1024), np.float32)
    for c in range(8):
        b, half = c // 2, c % 2
        o = np.asarray(res.results[c]["o"])
        out[b, half * TOK:(half + 1) * TOK, :] = o.T
    return out
